# version 01 (final)
# baseline (speedup 1.0000x reference)
import numpy as np
import ml_dtypes
from contextlib import ExitStack

import concourse.bass as bass
import concourse.mybir as mybir
from concourse.bass_utils import run_bass_kernel_spmd

F32 = mybir.dt.float32
BF16 = mybir.dt.bfloat16
AF = mybir.ActivationFunctionType
ALU = mybir.AluOpType
AX = mybir.AxisListType
bf16 = ml_dtypes.bfloat16

MAGIC = float(np.float32(1.5 * 2 ** 23))
T = 1024
D = 2048
DIN = 4096
HEADS = 64
HP = 64
CONVD = 4352
NCH = 4
CH = 256
DFF = 8192
EPS = 1e-5
NKD = 16
NOC = 34
NTC = 8
NEL_W = float(DFF * D)
GROUPS = [[0, 1], [2, 3], [4, 5], [6, 7]]


class Sem:
    def __init__(self, nc, st, name):
        self.h = st.enter_context(nc.semaphore(name))
        self.n = 0

    def inc(self, instr, k=1):
        instr.then_inc(self.h, k)
        self.n += k
        return instr


class B:
    pass


def _build(debug=False, phases=("a", "b", "c", "d", "e")):
    b = B()
    nc = bass.Bass("TRN2", target_bir_lowering=False)
    b.nc = nc
    b.debug = debug

    def din(name, shape, dt=F32):
        return nc.dram_tensor(name, shape, dt, kind="ExternalInput")

    b.x_d = din("x", [T, D])
    b.wgate_d = din("wgate", [8, 128, NKD * 512])
    b.wx_d = din("wx", [NOC, 128, NKD * 128])
    b.wdt_d = din("wdt", [128, NKD * HP])
    b.wop_d = din("wop", [4, 128, 32 * 512])
    b.wg_d = din("wg", [16, 128, NKD * 512])
    b.wu_d = din("wu", [16, 128, NKD * 512])
    b.wd_d = din("wd", [16, 128, 64 * 128])
    b.convw_d = din("convw", [128, NOC * 4])
    b.convb_d = din("convb", [128, NOC])
    b.dtbias_d = din("dtbias", [HEADS, 1])
    b.alog_d = din("alog", [HEADS, 1])
    b.drep_d = din("drep", [128, DIN])
    b.normwrep_d = din("normwrep", [128, DIN])
    b.normw_d = din("normw", [128, 32])
    b.ones2_d = din("ones2", [2, 128])
    b.identf_d = din("identf", [128, 128])
    b.gmask_d = din("gmask", [128, 512], BF16)
    b.bnd_d = din("bnd", [128, 1])

    b.out_d = nc.dram_tensor("out", [T, D], F32, kind="ExternalOutput")

    def dint(name, shape, dt=F32):
        return nc.dram_tensor(name, shape, dt, kind="Internal")

    b.sg_dram = dint("sg_dram", [T, DIN], BF16)
    b.act_dram = dint("act_dram", [CONVD, T], BF16)
    b.dacum_dram = dint("dacum_dram", [HEADS, T])
    b.xln_dram = dint("xln_dram", [T, D], BF16)
    b.x2_dram = dint("x2_dram", [T, D])
    b.xq2_dram = dint("xq2_dram", [T, D], BF16)
    b.yn_dram = dint("yn_dram", [T, DIN], BF16)
    b.h_dram = dint("h_dram", [T, DFF])
    b.hq_dram = dint("hq_dram", [T, DFF], BF16)
    b.dn_dram = dint("dn_dram", [D, T], BF16)
    b.sh_dram = dint("sh_dram", [NTC, 128])
    b.cc1_in = dint("cc1_in", [128, 102])
    b.cc1_out = dint("cc1_out", [256, 102])
    b.cc2_in = dint("cc2_in", [128, DIN], BF16)
    b.cc2_out = dint("cc2_out", [256, DIN], BF16)

    b.dbg = {}

    def mkdbg(name, shape, dt=F32):
        if debug:
            b.dbg[name] = nc.dram_tensor("dbg_" + name, shape, dt, kind="ExternalOutput")

    mkdbg("xln", [T, D], BF16)
    mkdbg("mcol", [128, 64])
    mkdbg("act", [CONVD, T], BF16)
    mkdbg("pc", [CONVD, T], BF16)
    mkdbg("cacc", [CONVD, T], BF16)
    mkdbg("pkr", [128, 102])
    mkdbg("pks", [128, 102])
    mkdbg("dts", [HEADS, T])
    mkdbg("dac", [HEADS, T])
    mkdbg("sg", [T, DIN], BF16)
    mkdbg("states", [128, NCH * DIN], BF16)
    mkdbg("prev4", [128, DIN], BF16)
    mkdbg("hg", [T, DIN], BF16)
    mkdbg("yn", [T, DIN], BF16)
    mkdbg("x2", [T, D])
    mkdbg("xq2", [T, D], BF16)
    mkdbg("h", [T, DFF])
    mkdbg("wsc", [1, 8])
    mkdbg("hq", [T, DFF], BF16)

    st = ExitStack()
    b.st = st
    with st:
        def sb(name, shape, dt=F32, stk=None):
            return (stk or st).enter_context(nc.sbuf_tensor(name, shape, dt))

        def ps(name, shape, dt=F32, stk=None):
            return (stk or st).enter_context(nc.psum_tensor(name, shape, dt))

        b.sb = sb
        b.ps = ps

        b.ones2 = sb("ones2s", [2, 128])
        b.identf = sb("identfs", [128, 128])
        b.gmask = sb("gmasks", [128, 512], BF16)
        b.convw = sb("convws", [128, NOC * 4])
        b.convb = sb("convbs", [128, NOC])
        b.dtbias = sb("dtbiass", [HEADS, 1])
        b.alog = sb("alogs", [HEADS, 1])

        b.normw = sb("normws", [128, 32])
        b.bnd = sb("bnds", [128, 1])
        b.acol = sb("acols", [HEADS, 1])
        b.etcrow = sb("etcrows", [2, HEADS])
        b.dacum_tm = sb("dacum_tms", [128, NTC * HEADS])
        b.dt_tm = sb("dt_tms", [128, NTC * HEADS])
        b.eda_tm = sb("eda_tms", [128, NTC * HEADS])
        b.ddc_tm = sb("ddc_tms", [128, NTC * HEADS])
        b.etcall = sb("etcalls", [128, NCH * HEADS])
        b.bmcm = sb("bmcms", [128, 2048], BF16)
        b.bm_tm = sb("bm_tms", [128, 1024], BF16)
        b.gm_all = sb("gm_alls", [128, NCH * 512], BF16)
        b.sx2c = sb("sx2cs", [128, NTC])
        b.wqcol = sb("wqcols", [128, 4])
        b.wscol = sb("wscols", [128, 4])
        b.wsc1 = sb("wsc1s", [2, 8])
        b.abscol = sb("abscols", [128, 4])
        b.onescol = sb("onescols", [128, 1])
        b.rmsc = sb("rmscs", [128, 4])
        b.invshc = sb("invshcs", [128, 8])
        b.vjunk = sb("vjunks", [1, 2])
        b.sjunk = sb("sjunks", [1, 2])

        def vbar(v):
            v.drain()
            return v.tensor_scalar(b.vjunk[0:1, 0:1], b.vjunk[0:1, 0:1], 1.0, None, ALU.mult)

        def sbar(s):
            s.drain()
            return s.activation(b.sjunk[0:1, 0:1], b.sjunk[0:1, 0:1], AF.Copy)
        b.vbar = vbar
        b.sbar = sbar

        def phase_gate():
            sems = [(b.stn, b.stn.n), (b.svc, b.svc.n), (b.ssc, b.ssc.n),
                    (b.sy, b.sy.n), (b.sc, b.sc.n), (b.sgq, b.sgq.n)]
            with nc.Block() as blk:
                for engname in ("sync", "scalar", "gpsimd", "tensor", "vector"):
                    def mk(eng, sems=sems):
                        for sem, val in sems:
                            eng.wait_ge(sem.h, val)
                    getattr(blk, engname)(mk)
        b.phase_gate = phase_gate

        b.sy = Sem(nc, st, "sy")
        b.sc = Sem(nc, st, "sc")
        b.sgq = Sem(nc, st, "sgq")
        b.stn = Sem(nc, st, "stn")
        b.svc = Sem(nc, st, "svc")
        b.ssc = Sem(nc, st, "ssc")
        b.scl = Sem(nc, st, "scl")

        _phase_0a(b)
        if "b" in phases:
            _phase_b(b)
        if "c" in phases:
            _phase_c(b)
        if "d" in phases:
            _phase_d(b)
        if "e" in phases:
            _phase_e(b)
        _finish(b, phases)
    return nc, b.dbg


def _phase_0a(b):
    nc, sb, ps = b.nc, b.sb, b.ps
    sy, sc, sgq, stn, svc, ssc, scl = b.sy, b.sc, b.sgq, b.stn, b.svc, b.ssc, b.scl

    ph_a = ExitStack()
    b.ph_a = ph_a
    x_in = sb("x_ins", [128, 2 * D], stk=ph_a)
    mcol = sb("mcols", [128, 64], stk=ph_a)
    xlnt = sb("xlnts", [128, 2 * D], BF16, stk=ph_a)

    with nc.Block() as blk:
        @blk.sync
        def _(sync):
            for dst, src in [(b.ones2, b.ones2_d), (b.identf, b.identf_d), (b.convw, b.convw_d),
                             (b.convb, b.convb_d), (b.dtbias, b.dtbias_d), (b.alog, b.alog_d),
                             (b.normw, b.normw_d), (b.bnd, b.bnd_d)]:
                sy.inc(sync.dma_start(out=dst[:, :], in_=src[:, :]), 16)
            sy.inc(sync.dma_start(out=x_in[:, 0:D], in_=b.x_d[0:128, :]), 16)

        @blk.gpsimd
        def _(gp):
            sgq.inc(gp.dma_start(out=b.gmask[:, :], in_=b.gmask_d[:, :]), 16)

        @blk.scalar
        def _(s):
            s.memzero(b.sjunk[:, :])

        @blk.vector
        def _(v):
            v.memset(b.vjunk[:, :], 1.0)
            v.memset(b.etcrow[:, :], 0.0)
            v.memset(b.wsc1[:, :], 0.0)
            v.memset(b.abscol[:, :], 0.0)
            svc.inc(v.memset(b.onescol[:, :], 1.0))

    b.lnx_done = []
    b.lnst = []
    for tc in range(NTC):
        pb = tc % 2
        o8 = tc * 8
        with nc.Block() as blk:
            @blk.scalar
            def _(s, tc=tc, pb=pb, o8=o8):
                s.wait_ge(sy.h, 16 * (9 + tc))
                if tc >= 2:
                    s.wait_ge(sc.h, b.lnst[tc - 2])
                xs = x_in[:, pb * D:(pb + 1) * D]
                s.activation(xlnt[:, pb * D:(pb + 1) * D], xs, AF.Square,
                             bias=0.0, scale=1.0, accum_out=mcol[:, o8 + 2:o8 + 3])
                ssc.inc(b.sbar(s))
            sq_done = ssc.n

            @blk.vector
            def _(v, tc=tc, pb=pb, sq_done=sq_done, o8=o8):
                v.wait_ge(sy.h, 16 * (9 + tc))
                xs = x_in[:, pb * D:(pb + 1) * D]
                v.reduce_sum(mcol[:, o8:o8 + 1], xs, AX.X)
                v.drain()
                v.wait_ge(ssc.h, sq_done)
                v.tensor_scalar(mcol[:, o8 + 3:o8 + 4], mcol[:, o8 + 2:o8 + 3], 1.0 / D, None, ALU.mult)
                v.tensor_scalar(mcol[:, o8 + 1:o8 + 2], mcol[:, o8:o8 + 1], 1.0 / D, None, ALU.mult)
                v.drain()
                v.tensor_tensor(mcol[:, o8 + 4:o8 + 5], mcol[:, o8 + 1:o8 + 2],
                                mcol[:, o8 + 1:o8 + 2], ALU.mult)
                v.drain()
                v.tensor_tensor(mcol[:, o8 + 5:o8 + 6], mcol[:, o8 + 3:o8 + 4],
                                mcol[:, o8 + 4:o8 + 5], ALU.subtract)
                v.drain()
                v.tensor_scalar(mcol[:, o8 + 7:o8 + 8], mcol[:, o8 + 5:o8 + 6], EPS, None, ALU.add)
                v.drain()
                v.reciprocal(mcol[:, o8 + 6:o8 + 7], mcol[:, o8 + 7:o8 + 8])
                svc.inc(b.vbar(v))
            var_done = svc.n

            @blk.scalar
            def _(s, var_done=var_done, o8=o8):
                s.wait_ge(svc.h, var_done)
                s.activation(mcol[:, o8 + 5:o8 + 6], mcol[:, o8 + 6:o8 + 7], AF.Sqrt)
                ssc.inc(b.sbar(s))
            rstd_done = ssc.n

            @blk.vector
            def _(v, tc=tc, pb=pb, rstd_done=rstd_done, o8=o8):
                v.wait_ge(ssc.h, rstd_done)
                xs = x_in[:, pb * D:(pb + 1) * D]
                svc.inc(v.scalar_tensor_tensor(
                    xlnt[:, pb * D:(pb + 1) * D], xs, mcol[:, o8 + 1:o8 + 2],
                    mcol[:, o8 + 5:o8 + 6].broadcast_to((128, D)), ALU.subtract, ALU.mult))
            ln_done = svc.n

            @blk.scalar
            def _(s, tc=tc, pb=pb, ln_done=ln_done):
                s.wait_ge(svc.h, ln_done)
                sc.inc(s.dma_start(out=b.xln_dram[tc * 128:(tc + 1) * 128, :],
                                   in_=xlnt[:, pb * D:(pb + 1) * D]), 16)
            b.lnst.append(sc.n)

            @blk.sync
            def _(sync, tc=tc, pb=pb, ln_done=ln_done):
                if tc + 1 < NTC:
                    if tc >= 1:
                        sync.wait_ge(svc.h, b.lnx_done[tc - 1])
                    sy.inc(sync.dma_start(
                        out=x_in[:, (1 - pb) * D:(2 - pb) * D],
                        in_=b.x_d[(tc + 1) * 128:(tc + 2) * 128, :]), 16)
            b.lnx_done.append(ln_done)
    ln1_wr = sc.n

    xln_fm = sb("xln_fms", [128, NKD * T], BF16, stk=ph_a)
    b.xln_fm = xln_fm
    with nc.Block() as blk:
        @blk.sync
        def _(sync):
            sync.wait_ge(sc.h, ln1_wr)
            for k in range(NKD):
                sy.inc(sync.dma_start_transpose(
                    xln_fm[:, k * T:(k + 1) * T],
                    b.xln_dram[:, k * 128:(k + 1) * 128]), 16)
    b.xlnfm_rdy = sy.n

    if b.debug:
        with nc.Block() as blk:
            @blk.scalar
            def _(s):
                s.wait_ge(sc.h, ln1_wr)
                s.wait_ge(svc.h, svc.n)
                sc.inc(s.dma_start(out=b.dbg["mcol"][:, :], in_=mcol[:, :]), 16)
                for tc in range(NTC):
                    sc.inc(s.dma_start(out=b.dbg["xln"][tc * 128:(tc + 1) * 128, :],
                                       in_=b.xln_dram[tc * 128:(tc + 1) * 128, :]), 16)

    wdt_sb = sb("wdt_sbs", [128, NKD * HP], BF16, stk=ph_a)
    dwork = sb("dworks", [HEADS, T], stk=ph_a)
    dts = sb("dtss", [HEADS, T], stk=ph_a)
    dacum = sb("dacums", [HEADS, T], stk=ph_a)
    onesd = sb("onesds", [HEADS, CH], stk=ph_a)
    ph_dt = ExitStack()
    ps_dt = ps("ps_dts", [HEADS, T], stk=ph_dt)
    ps_tr = ps("ps_trs", [128, 64], stk=ph_dt)

    with nc.Block() as blk:
        @blk.gpsimd
        def _(gp):
            sgq.inc(gp.dma_start(out=wdt_sb[:, :], in_=b.wdt_d.ap()), 16)
        wdt_ld = sgq.n

        @blk.tensor
        def _(t, wdt_ld=wdt_ld):
            t.wait_ge(sy.h, b.xlnfm_rdy)
            t.wait_ge(sgq.h, wdt_ld)
            for k in range(NKD):
                t.matmul(ps_dt[:, 0:512], wdt_sb[:, k * HP:(k + 1) * HP],
                         xln_fm[:, k * T:k * T + 512],
                         start=(k == 0), stop=(k == NKD - 1))
                last = t.matmul(ps_dt[:, 512:1024], wdt_sb[:, k * HP:(k + 1) * HP],
                                xln_fm[:, k * T + 512:(k + 1) * T],
                                start=(k == 0), stop=(k == NKD - 1))
            stn.inc(last)
        dtmm = stn.n

        @blk.scalar
        def _(s, dtmm=dtmm):
            s.wait_ge(stn.h, dtmm)
            s.activation(dwork[:, :], ps_dt[:, :], AF.Identity, bias=b.dtbias[:, 0:1], scale=1.0)
            s.activation(dts[:, :], dwork[:, :], AF.Abs)
            s.activation(dts[:, :], dts[:, :], AF.Exp, bias=0.0, scale=-1.0)
            s.activation(dts[:, :], dts[:, :], AF.Ln, bias=1.0)
            s.activation(dacum[:, :], dwork[:, :], AF.Relu)
            s.activation(b.acol[:, 0:1], b.alog[:, 0:1], AF.Exp)
            ssc.inc(b.sbar(s))
            ssc.inc(b.sbar(s))
        sp_done = ssc.n

        @blk.vector
        def _(v, sp_done=sp_done):
            v.wait_ge(ssc.h, sp_done)
            v.memset(onesd[:, :], 1.0)
            v.tensor_tensor(dts[:, :], dts[:, :], dacum[:, :], ALU.add)
            v.tensor_scalar(b.acol[:, 0:1], b.acol[:, 0:1], -1.0, None, ALU.mult)
            v.drain()
            v.tensor_scalar(dwork[:, :], dts[:, :], b.acol[:, 0:1], None, ALU.mult)
            for c in range(NCH):
                sl = slice(c * CH, (c + 1) * CH)
                svc.inc(v.tensor_tensor_scan(dacum[:, sl], onesd[:, :], dwork[:, sl],
                                             0.0, ALU.mult, ALU.add))
        dac_v = svc.n
        b.dac_v = dac_v

        @blk.scalar
        def _(s, dac_v=dac_v):
            s.wait_ge(svc.h, dac_v)
            sc.inc(s.dma_start(out=b.dacum_dram[:, :], in_=dacum[:, :]), 16)
        b.dacum_wr = sc.n

    if b.debug:
        with nc.Block() as blk:
            @blk.scalar
            def _(s):
                s.wait_ge(svc.h, b.dac_v)
                sc.inc(s.dma_start(out=b.dbg["dts"][:, :], in_=dts[:, :]), 16)
                sc.inc(s.dma_start(out=b.dbg["dac"][:, :], in_=dacum[:, :]), 16)

    for i in range(NTC):
        with nc.Block() as blk:
            @blk.tensor
            def _(t, i=i):
                t.wait_ge(svc.h, svc.n)
                stn.inc(t.transpose(ps_tr[:, :], dacum[:, i * 128:(i + 1) * 128], b.identf[0:64, 0:64]))
            t1 = stn.n

            @blk.vector
            def _(v, t1=t1, i=i):
                v.wait_ge(stn.h, t1)
                svc.inc(v.tensor_copy(b.dacum_tm[:, i * HEADS:(i + 1) * HEADS], ps_tr[:, 0:HEADS]))
            v1 = svc.n

            @blk.tensor
            def _(t, v1=v1, i=i):
                t.wait_ge(svc.h, v1)
                stn.inc(t.transpose(ps_tr[:, :], dts[:, i * 128:(i + 1) * 128], b.identf[0:64, 0:64]))
            t2 = stn.n

            @blk.vector
            def _(v, t2=t2, i=i):
                v.wait_ge(stn.h, t2)
                svc.inc(v.tensor_copy(b.dt_tm[:, i * HEADS:(i + 1) * HEADS], ps_tr[:, 0:HEADS]))

    with nc.Block() as blk:
        @blk.scalar
        def _(s):
            s.wait_ge(svc.h, svc.n)
            ssc.inc(s.activation(b.eda_tm[:, :], b.dacum_tm[:, :], AF.Exp))

        @blk.vector
        def _(v):
            for c in range(NCH):
                tcc = dacum[:, c * CH + 255:c * CH + 256]
                svc.inc(v.scalar_tensor_tensor(
                    dwork[:, c * CH:(c + 1) * CH], dacum[:, c * CH:(c + 1) * CH],
                    -1.0, tcc.broadcast_to((HEADS, CH)), ALU.mult, ALU.add))
        dd = svc.n

        @blk.scalar
        def _(s, dd=dd):
            s.wait_ge(svc.h, dd)
            ssc.inc(s.activation(dwork[:, :], dwork[:, :], AF.Exp))
        b.dec_exp = ssc.n

    for i in range(NTC):
        with nc.Block() as blk:
            @blk.tensor
            def _(t, i=i):
                t.wait_ge(ssc.h, b.dec_exp)
                t.wait_ge(svc.h, svc.n)
                stn.inc(t.transpose(ps_tr[:, :], dwork[:, i * 128:(i + 1) * 128], b.identf[0:64, 0:64]))
            t1 = stn.n

            @blk.vector
            def _(v, t1=t1, i=i):
                v.wait_ge(stn.h, t1)
                svc.inc(v.tensor_tensor(b.ddc_tm[:, i * HEADS:(i + 1) * HEADS],
                                        ps_tr[:, 0:HEADS],
                                        b.dt_tm[:, i * HEADS:(i + 1) * HEADS], ALU.mult))

    for c in range(NCH):
        with nc.Block() as blk:
            @blk.scalar
            def _(s, c=c):
                s.wait_ge(sc.h, b.dacum_wr)
                s.wait_ge(svc.h, svc.n)
                sc.inc(s.dma_start(out=b.etcrow[0:1, :],
                                   in_=b.dacum_dram[:, c * CH + 255:c * CH + 256]
                                   .rearrange("h o -> o h"),
                                   allow_slow_non_contiguous=True), 16)
                s.wait_ge(sc.h, sc.n)
                ssc.inc(s.activation(b.etcrow[0:1, :], b.etcrow[0:1, :], AF.Exp))
            e1 = ssc.n

            @blk.tensor
            def _(t, e1=e1):
                t.wait_ge(ssc.h, e1)
                stn.inc(t.matmul(ps_tr[:, 0:64], b.ones2[:, :], b.etcrow[:, :],
                                 start=True, stop=True))
            t1 = stn.n

            @blk.vector
            def _(v, t1=t1, c=c):
                v.wait_ge(stn.h, t1)
                svc.inc(v.tensor_copy(b.etcall[:, c * HEADS:(c + 1) * HEADS], ps_tr[:, 0:64]))
    ph_dt.close()
    b.dt_done_v = svc.n

    wx_sb = sb("wx_sbs", [128, 2 * NKD * 128], BF16, stk=ph_a)
    pconv = sb("pconvs", [128, 2 * 1027], BF16, stk=ph_a)
    cacc = sb("caccs", [128, 2 * T], BF16, stk=ph_a)
    actt = sb("actts", [128, 2 * T], BF16, stk=ph_a)
    pk_send = sb("pk_sends", [128, 102], stk=ph_a)
    pk_own = sb("pk_owns", [128, 102], stk=ph_a)
    pk_recv = sb("pk_recvs", [128, 102], stk=ph_a)
    fixt = sb("fixts", [128, 102], stk=ph_a)
    fixs = sb("fixss", [128, 102], BF16, stk=ph_a)
    ph_wab = ExitStack()
    wab = sb("wabs", [128, 2 * 4096], stk=ph_wab)
    ph_xbc = ExitStack()
    ps_x = ps("ps_xs", [128, 2 * T], stk=ph_xbc)

    import os
    if os.environ.get("NO_P1") == "1":
        pass1_jobs = []
    else:
        pass1_jobs = [(b.wg_d, i // 2, 0, i % 2) for i in range(32)] + \
                     [(b.wu_d, i // 2, 1, i % 2) for i in range(32)] + \
                     [(b.wd_d, i // 2, 2, (i % 2)) for i in range(32)]

    with nc.Block() as blk:
        @blk.vector
        def _(v):
            v.memset(pconv[:, 0:3], 0.0)
            svc.inc(v.memset(pconv[:, 1027:1030], 0.0))

    mm_done = []
    evac_done = []
    conv_done = []
    silu_done = []
    act_written = []
    wx_loaded = []
    p1_loaded = []
    p1_abs = []
    p1_acc = []

    def emit_pass1_load(sync, j):
        job = pass1_jobs[j]
        if j >= 2:
            sync.wait_ge(ssc.h, p1_abs[j - 2])
        wofj = (j % 2) * 4096
        half = job[3]
        sy.inc(sync.dma_start(out=wab[:, wofj:wofj + 4096],
                              in_=job[0][job[1]][:, half * 4096:(half + 1) * 4096]), 16)
        p1_loaded.append(sy.n)

    with nc.Block() as blk:
        @blk.gpsimd
        def _(gp):
            for oc in (0, 1):
                sgq.inc(gp.dma_start(out=wx_sb[:, oc * 2048:(oc + 1) * 2048], in_=b.wx_d[oc]), 16)
                wx_loaded.append(sgq.n)

        @blk.sync
        def _(sync):
            for j in (0, 1):
                if j < len(pass1_jobs):
                    emit_pass1_load(sync, j)

    for oc in range(NOC):
        pb = oc % 2
        with nc.Block() as blk:
            @blk.tensor
            def _(t, oc=oc, pb=pb):
                t.wait_ge(sy.h, b.xlnfm_rdy)
                t.wait_ge(sgq.h, wx_loaded[oc])
                import os
                if os.environ.get("NO_PIPE") == "1" and oc >= 1:
                    t.wait_ge(ssc.h, silu_done[oc - 1])
                if oc >= 2:
                    t.wait_ge(svc.h, evac_done[oc - 2])
                wof = pb * 2048
                pof = pb * T
                for k in range(NKD):
                    t.matmul(ps_x[:, pof:pof + 512], wx_sb[:, wof + k * 128:wof + (k + 1) * 128],
                             xln_fm[:, k * T:k * T + 512],
                             start=(k == 0), stop=(k == NKD - 1))
                    last = t.matmul(ps_x[:, pof + 512:pof + T],
                                    wx_sb[:, wof + k * 128:wof + (k + 1) * 128],
                                    xln_fm[:, k * T + 512:(k + 1) * T],
                                    start=(k == 0), stop=(k == NKD - 1))
                stn.inc(last)
            mm_done.append(stn.n)

            @blk.gpsimd
            def _(gp, oc=oc):
                if oc + 2 < NOC:
                    gp.wait_ge(stn.h, mm_done[oc])
                    sgq.inc(gp.dma_start(out=wx_sb[:, (oc % 2) * 2048:(oc % 2 + 1) * 2048],
                                         in_=b.wx_d[oc + 2]), 16)
                    wx_loaded.append(sgq.n)

            @blk.vector
            def _(v, oc=oc, pb=pb):
                v.wait_ge(stn.h, mm_done[oc])
                if oc >= 2:
                    v.wait_ge(ssc.h, silu_done[oc - 2])
                pof = pb * T
                pcb = pb * 1027
                cob = pb * T
                v.tensor_scalar(pconv[:, pcb + 3:pcb + 1027], ps_x[:, pof:pof + T],
                                1.0, None, ALU.mult)
                v.drain()
                v.tensor_copy(pk_send[:, oc * 3:(oc + 1) * 3], pconv[:, pcb + 1024:pcb + 1027])
                svc.inc(v.tensor_copy(pk_own[:, oc * 3:(oc + 1) * 3], pconv[:, pcb + 3:pcb + 6]))
                evac_done.append(svc.n)
                v.scalar_tensor_tensor(cacc[:, cob:cob + T], pconv[:, pcb + 0:pcb + T],
                                       b.convw[:, oc * 4:oc * 4 + 1],
                                       b.convb[:, oc:oc + 1].broadcast_to((128, T)),
                                       ALU.mult, ALU.add)
                for i in (1, 2):
                    v.scalar_tensor_tensor(cacc[:, cob:cob + T], pconv[:, pcb + i:pcb + i + T],
                                           b.convw[:, oc * 4 + i:oc * 4 + i + 1],
                                           cacc[:, cob:cob + T], ALU.mult, ALU.add)
                svc.inc(v.scalar_tensor_tensor(cacc[:, cob:cob + T], pconv[:, pcb + 3:pcb + 3 + T],
                                               b.convw[:, oc * 4 + 3:oc * 4 + 4],
                                               cacc[:, cob:cob + T], ALU.mult, ALU.add))
                conv_done.append(svc.n)

            @blk.scalar
            def _(s, oc=oc, pb=pb):
                s.wait_ge(svc.h, conv_done[oc])
                if oc >= 2:
                    s.wait_ge(sc.h, act_written[oc - 2])
                cob = pb * T
                if b.debug:
                    sc.inc(s.dma_start(out=b.dbg["cacc"][oc * 128:(oc + 1) * 128, :],
                                       in_=cacc[:, cob:cob + T]), 16)
                s.activation(actt[:, cob:cob + T], cacc[:, cob:cob + T], AF.Silu)
                s.drain()
                ssc.inc(b.sbar(s))
                silu_done.append(ssc.n)
                sc.inc(s.dma_start(out=b.act_dram[oc * 128:(oc + 1) * 128, :],
                                   in_=actt[:, cob:cob + T]), 16)
                if b.debug:
                    pcb2 = (oc % 2) * 1027
                    sc.inc(s.dma_start(out=b.dbg["pc"][oc * 128:(oc + 1) * 128, :],
                                       in_=pconv[:, pcb2 + 3:pcb2 + 1027]), 16)
                act_written.append(sc.n)

            jbase = 2 + 2 * oc
            for j in (jbase - 2, jbase - 1):
                if 0 <= j < len(pass1_jobs):
                    @blk.scalar
                    def _(s, j=j):
                        s.wait_ge(sy.h, p1_loaded[j])
                        if j >= 1:
                            s.wait_ge(svc.h, p1_acc[j - 1])
                        wofj = (j % 2) * 4096
                        s.activation(wab[:, wofj:wofj + 4096], wab[:, wofj:wofj + 4096],
                                     AF.Abs, bias=0.0, scale=1.0,
                                     accum_out=b.abscol[:, 3:4])
                        ssc.inc(b.sbar(s))
                        p1_abs.append(ssc.n)

                    @blk.vector
                    def _(v, j=j):
                        w = pass1_jobs[j][2]
                        v.wait_ge(ssc.h, p1_abs[j])
                        v.tensor_tensor(b.abscol[:, w:w + 1], b.abscol[:, w:w + 1],
                                        b.abscol[:, 3:4], ALU.add)
                        svc.inc(b.vbar(v))
                        p1_acc.append(svc.n)

            for j in (jbase, jbase + 1):
                if j < len(pass1_jobs):
                    @blk.sync
                    def _(sync, j=j):
                        emit_pass1_load(sync, j)

    for j in range(2 + 2 * NOC - 2, len(pass1_jobs)):
        with nc.Block() as blk:
            @blk.scalar
            def _(s, j=j):
                s.wait_ge(sy.h, p1_loaded[j])
                if j >= 1:
                    s.wait_ge(svc.h, p1_acc[j - 1])
                wofj = (j % 2) * 4096
                s.activation(wab[:, wofj:wofj + 4096], wab[:, wofj:wofj + 4096],
                             AF.Abs, bias=0.0, scale=1.0, accum_out=b.abscol[:, 3:4])
                ssc.inc(b.sbar(s))
                p1_abs.append(ssc.n)

            @blk.vector
            def _(v, j=j):
                w = pass1_jobs[j][2]
                v.wait_ge(ssc.h, p1_abs[j])
                v.tensor_tensor(b.abscol[:, w:w + 1], b.abscol[:, w:w + 1],
                                b.abscol[:, 3:4], ALU.add)
                svc.inc(b.vbar(v))
                p1_acc.append(svc.n)

            if j + 2 < len(pass1_jobs):
                @blk.sync
                def _(sync, j=j):
                    emit_pass1_load(sync, j + 2)

    with nc.Block() as blk:
        @blk.scalar
        def _(s):
            s.wait_ge(svc.h, evac_done[-1])
            sc.inc(s.dma_start(out=b.cc1_in.ap(), in_=pk_send[:, :]), 16)
        cc1_wr = sc.n

        @blk.gpsimd
        def _(gp, cc1_wr=cc1_wr):
            gp.wait_ge(sc.h, cc1_wr)
            scl.inc(gp.collective_compute(
                "AllGather", ALU.bypass, replica_groups=GROUPS,
                ins=[b.cc1_in.ap()], outs=[b.cc1_out.ap()]))
            gp.wait_ge(scl.h, scl.n)
            sgq.inc(gp.dma_start(out=pk_recv[:, :], in_=b.cc1_out.ap()[0:128, :]), 16)
        cc1_rd = sgq.n

        @blk.scalar
        def _(s, cc1_rd=cc1_rd):
            if b.debug:
                s.wait_ge(sgq.h, cc1_rd)
                sc.inc(s.dma_start(out=b.dbg["pkr"][:, :], in_=pk_recv[:, :]), 16)
                sc.inc(s.dma_start(out=b.dbg["pks"][:, :], in_=pk_send[:, :]), 16)

        @blk.vector
        def _(v, cc1_rd=cc1_rd):
            v.wait_ge(sgq.h, cc1_rd)
            if b.debug:
                v.wait_ge(sc.h, sc.n)
            v.tensor_scalar(pk_recv[:, :], pk_recv[:, :], b.bnd[:, 0:1], None, ALU.mult)
            ins = None
            for oc in range(NOC):
                o3 = oc * 3
                for tt in range(3):
                    for i in range(4):
                        w = tt + i
                        src = (pk_recv[:, o3 + w:o3 + w + 1] if w < 3
                               else pk_own[:, o3 + w - 3:o3 + w - 2])
                        if i == 0:
                            ins = v.scalar_tensor_tensor(
                                fixt[:, o3 + tt:o3 + tt + 1], src,
                                b.convw[:, oc * 4:oc * 4 + 1],
                                b.convb[:, oc:oc + 1], ALU.mult, ALU.add)
                        else:
                            ins = v.scalar_tensor_tensor(
                                fixt[:, o3 + tt:o3 + tt + 1], src,
                                b.convw[:, oc * 4 + i:oc * 4 + i + 1],
                                fixt[:, o3 + tt:o3 + tt + 1], ALU.mult, ALU.add)
            svc.inc(ins)
        fix_v = svc.n

        @blk.scalar
        def _(s, fix_v=fix_v):
            s.wait_ge(svc.h, fix_v)
            s.activation(fixs[:, :], fixt[:, :], AF.Silu)
            s.drain()
            ssc.inc(b.sbar(s))
            s.wait_ge(sc.h, act_written[-1])
            for oc in range(NOC):
                sc.inc(s.dma_start(out=b.act_dram[oc * 128:(oc + 1) * 128, 0:3],
                                   in_=fixs[:, oc * 3:(oc + 1) * 3]), 16)
        fix_s = ssc.n

        @blk.vector
        def _(v, fix_s=fix_s):
            v.wait_ge(ssc.h, silu_done[32])
            v.tensor_copy(b.bmcm[:, 0:T], actt[:, 0:T])
            v.wait_ge(ssc.h, silu_done[33])
            v.tensor_copy(b.bmcm[:, T:2 * T], actt[:, T:2 * T])
            v.wait_ge(ssc.h, fix_s)
            v.tensor_copy(b.bmcm[:, 0:3], fixs[:, 96:99])
            svc.inc(v.tensor_copy(b.bmcm[:, T:T + 3], fixs[:, 99:102]))
    ph_xbc.close()
    b.actfix_done = sc.n
    b.bmcm_done = svc.n

    if b.debug:
        with nc.Block() as blk:
            @blk.sync
            def _(sync):
                sync.wait_ge(sc.h, b.actfix_done)
                for i in range(NOC):
                    sy.inc(sync.dma_start(out=b.dbg["act"][i * 128:(i + 1) * 128, :],
                                          in_=b.act_dram[i * 128:(i + 1) * 128, :]), 16)

    with nc.Block() as blk:
        @blk.sync
        def _(sync):
            sync.wait_ge(svc.h, b.bmcm_done)
            for i in range(NTC):
                sy.inc(sync.dma_start_transpose(b.bm_tm[:, i * 128:(i + 1) * 128],
                                                b.bmcm[:, i * 128:(i + 1) * 128]), 16)
    b.bmtm_rdy = sy.n

    ph_gm = ExitStack()
    ps_gm = ps("ps_gms", [128, 512], stk=ph_gm)
    for c in range(NCH):
        for sh in range(2):
            with nc.Block() as blk:
                @blk.tensor
                def _(t, c=c, sh=sh):
                    t.wait_ge(svc.h, svc.n)
                    t.wait_ge(svc.h, b.bmcm_done)
                    stn.inc(t.matmul(ps_gm[:, sh * 256:(sh + 1) * 256],
                                     b.bmcm[:, c * 256 + sh * 128:c * 256 + (sh + 1) * 128],
                                     b.bmcm[:, T + c * 256:T + (c + 1) * 256],
                                     start=True, stop=True))
                t1 = stn.n

                @blk.vector
                def _(v, t1=t1, c=c, sh=sh):
                    v.wait_ge(stn.h, t1)
                    svc.inc(v.tensor_tensor(
                        b.gm_all[:, c * 512 + sh * 256:c * 512 + (sh + 1) * 256],
                        ps_gm[:, sh * 256:(sh + 1) * 256],
                        b.gmask[:, sh * 256:(sh + 1) * 256], ALU.mult))
    ph_gm.close()

    b.phase_gate()
    ph_wab.close()
    wg_sb = sb("wg_sbs", [128, 2 * NKD * 512], BF16, stk=ph_a)
    sgst = sb("sgsts", [128, 2 * 2048], BF16, stk=ph_a)
    ph_g = ExitStack()
    ps_g = ps("ps_gs", [128, 4096], stk=ph_g)
    wgld = []
    gmm = []
    gev = []
    gwr = []
    with nc.Block() as blk:
        @blk.gpsimd
        def _(gp):
            for ob in (0, 1):
                sgq.inc(gp.dma_start(out=wg_sb[:, ob * 8192:(ob + 1) * 8192],
                                     in_=b.wgate_d[ob]), 16)
                wgld.append(sgq.n)

    for ob in range(8):
        pb = ob % 2
        for tch in range(2):
            with nc.Block() as blk:
                @blk.tensor
                def _(t, ob=ob, pb=pb, tch=tch):
                    t.wait_ge(sgq.h, wgld[ob])
                    li = len(gev)
                    if li >= 2:
                        t.wait_ge(ssc.h, gev[li - 2])
                    wof = pb * 8192
                    for k in range(NKD):
                        for tq in range(4):
                            tcc = tch * 4 + tq
                            last = t.matmul(
                                ps_g[:, tcc * 512:(tcc + 1) * 512],
                                xln_fm[:, k * T + tcc * 128:k * T + (tcc + 1) * 128],
                                wg_sb[:, wof + k * 512:wof + (k + 1) * 512],
                                start=(k == 0), stop=(k == NKD - 1))
                    stn.inc(last)
                gmm.append(stn.n)

                @blk.gpsimd
                def _(gp, ob=ob, tch=tch):
                    if tch == 1 and ob + 2 < 8:
                        gp.wait_ge(stn.h, gmm[-1])
                        sgq.inc(gp.dma_start(out=wg_sb[:, (ob % 2) * 8192:(ob % 2 + 1) * 8192],
                                             in_=b.wgate_d[ob + 2]), 16)
                        wgld.append(sgq.n)

                @blk.scalar
                def _(s, ob=ob, pb=pb, tch=tch):
                    li = len(gev)
                    s.wait_ge(stn.h, gmm[li])
                    if li >= 2:
                        s.wait_ge(sc.h, gwr[li - 2])
                    sof = (li % 2) * 2048
                    s.activation(sgst[:, sof:sof + 2048], ps_g[:, tch * 2048:(tch + 1) * 2048],
                                 AF.Silu)
                    s.drain()
                    ssc.inc(b.sbar(s))
                    gev.append(ssc.n)
                    for tq in range(4):
                        tcc = tch * 4 + tq
                        sc.inc(s.dma_start(
                            out=b.sg_dram[tcc * 128:(tcc + 1) * 128, ob * 512:(ob + 1) * 512],
                            in_=sgst[:, sof + tq * 512:sof + (tq + 1) * 512]), 16)
                    gwr.append(sc.n)
    ph_g.close()
    b.sg_written = sc.n

    if b.debug:
        with nc.Block() as blk:
            @blk.sync
            def _(sync):
                sync.wait_ge(sc.h, b.sg_written)
                for i in range(NTC):
                    sy.inc(sync.dma_start(out=b.dbg["sg"][i * 128:(i + 1) * 128, :],
                                          in_=b.sg_dram[i * 128:(i + 1) * 128, :]), 16)

    b.phase_gate()
    ph_ws = ExitStack()
    ps_ws = ps("ps_wss", [1, 8], stk=ph_ws)
    ps_wb = ps("ps_wbs", [128, 4], stk=ph_ws)
    with nc.Block() as blk:
        @blk.tensor
        def _(t):
            t.wait_ge(svc.h, svc.n)
            for w in range(3):
                stn.inc(t.matmul(ps_ws[:, w:w + 1], b.abscol[:, w:w + 1], b.onescol[:, :],
                                 start=True, stop=True))
        ws_t = stn.n

        @blk.vector
        def _(v, ws_t=ws_t):
            v.wait_ge(stn.h, ws_t)
            v.tensor_scalar(b.wsc1[0:1, 0:3], ps_ws[0:1, 0:3], 1.0 / NEL_W, EPS, ALU.mult, ALU.add)
            v.drain()
            v.reciprocal(b.wsc1[0:1, 4:7], b.wsc1[0:1, 0:3])
            v.drain()
            svc.inc(v.tensor_scalar(b.wsc1[0:1, 3:4], b.wsc1[0:1, 3:4], 1.0, None, ALU.mult))
        ws_v = svc.n

        @blk.tensor
        def _(t, ws_v=ws_v):
            t.wait_ge(svc.h, ws_v)
            stn.inc(t.matmul(ps_wb[:, 0:4], b.ones2[:, :], b.wsc1[:, 4:8], start=True, stop=True))
        ws_t2 = stn.n

        @blk.vector
        def _(v, ws_t2=ws_t2):
            v.wait_ge(stn.h, ws_t2)
            svc.inc(v.tensor_copy(b.wqcol[:, :], ps_wb[:, :]))
        ws_v2 = svc.n

        @blk.tensor
        def _(t, ws_v2=ws_v2):
            t.wait_ge(svc.h, ws_v2)
            stn.inc(t.matmul(ps_wb[:, 0:4], b.ones2[:, :], b.wsc1[:, 0:4], start=True, stop=True))
        ws_t3 = stn.n

        @blk.vector
        def _(v, ws_t3=ws_t3):
            v.wait_ge(stn.h, ws_t3)
            svc.inc(v.tensor_copy(b.wscol[:, :], ps_wb[:, :]))
    ph_ws.close()
    b.wq_ready = svc.n

    if b.debug:
        with nc.Block() as blk:
            @blk.scalar
            def _(s):
                s.wait_ge(svc.h, b.wq_ready)
                sc.inc(s.dma_start(out=b.dbg["wsc"][:, :], in_=b.wsc1[0:1, :]), 16)
    ph_a.close()


def _phase_b(b):
    nc, sb, ps = b.nc, b.sb, b.ps
    sy, sc, sgq, stn, svc, ssc, scl = b.sy, b.sc, b.sgq, b.stn, b.svc, b.ssc, b.scl

    b.phase_gate()
    ph_b = ExitStack()
    b.ph_b = ph_b
    b.drep = sb("dreps", [128, DIN], BF16, stk=ph_b)
    b.normwrep = sb("normwreps", [128, DIN], BF16, stk=ph_b)
    b.flatbuf = sb("flatbufs", [2, 2 * 2048], stk=ph_b)
    xact_tm = sb("xact_tms", [128, 2 * DIN], BF16, stk=ph_b)
    xd_tm = sb("xd_tms", [128, 2 * DIN], BF16, stk=ph_b)
    xdd = sb("xdds", [128, 2 * DIN], BF16, stk=ph_b)
    states_sb = sb("states_sbs", [128, NCH * DIN], BF16, stk=ph_b)
    prev = sb("prevs", [128, DIN], BF16, stk=ph_b)
    tmpb = sb("tmpbs", [128, DIN], BF16, stk=ph_b)

    with nc.Block() as blk:
        @blk.gpsimd
        def _(gp):
            sgq.inc(gp.dma_start(out=b.drep[:, :], in_=b.drep_d[:, :]), 16)
            sgq.inc(gp.dma_start(out=b.normwrep[:, :], in_=b.normwrep_d[:, :]), 16)

        @blk.vector
        def _(v):
            svc.inc(v.memset(b.flatbuf[:, :], 0.0))
    b.bweights_ld = sgq.n

    ph_b1 = ExitStack()
    ps_s = ps("ps_ss", [128, 1024], stk=ph_b1)
    sev_done = []
    for c in range(NCH):
        with nc.Block() as blk:
            @blk.sync
            def _(sync, c=c):
                sync.wait_ge(sc.h, b.actfix_done)
                if c >= 1:
                    sync.wait_ge(svc.h, sev_done[-1])
                for sh in range(2):
                    sy.inc(sync.dma_start_transpose(
                        xact_tm[:, sh * DIN:(sh + 1) * DIN],
                        b.act_dram[0:DIN, c * CH + sh * 128:c * CH + (sh + 1) * 128]), 16)
            xr = sy.n

            @blk.vector
            def _(v, c=c, xr=xr):
                v.wait_ge(sy.h, xr)
                v.wait_ge(svc.h, b.dt_done_v)
                for sh in range(2):
                    tof = (2 * c + sh) * HEADS
                    svc.inc(v.tensor_tensor(
                        xdd[:, sh * DIN:(sh + 1) * DIN].rearrange("p (h q) -> p h q", q=HP),
                        xact_tm[:, sh * DIN:(sh + 1) * DIN].rearrange("p (h q) -> p h q", q=HP),
                        b.ddc_tm[:, tof:tof + HEADS].unsqueeze(2).broadcast_to((128, HEADS, HP)),
                        ALU.mult))
            xddv = svc.n

            smm = []
            for g in range(8):
                @blk.tensor
                def _(t, c=c, g=g, xddv=xddv):
                    t.wait_ge(svc.h, xddv)
                    t.wait_ge(sy.h, b.bmtm_rdy)
                    pg = g % 2
                    if g >= 2:
                        t.wait_ge(svc.h, sev_done[-2])
                    for sh in range(2):
                        last = t.matmul(ps_s[:, pg * 512:(pg + 1) * 512],
                                        b.bm_tm[:, (2 * c + sh) * 128:(2 * c + sh + 1) * 128],
                                        xdd[:, sh * DIN + g * 512:sh * DIN + (g + 1) * 512],
                                        start=(sh == 0), stop=(sh == 1))
                    stn.inc(last)
                smm.append(stn.n)

                @blk.vector
                def _(v, c=c, g=g, smmv=smm[g]):
                    pg = g % 2
                    v.wait_ge(stn.h, smmv)
                    svc.inc(v.tensor_scalar(
                        states_sb[:, c * DIN + g * 512:c * DIN + (g + 1) * 512],
                        ps_s[:, pg * 512:(pg + 1) * 512], 1.0, None, ALU.mult))
                sev_done.append(svc.n)
    ph_b1.close()

    with nc.Block() as blk:
        @blk.vector
        def _(v):
            v.wait_ge(svc.h, svc.n)
            v.memset(prev[:, :], 0.0)
            for c in range(NCH):
                v.tensor_tensor(tmpb[:, :].rearrange("p (h q) -> p h q", q=HP),
                                prev[:, :].rearrange("p (h q) -> p h q", q=HP),
                                b.etcall[:, c * HEADS:(c + 1) * HEADS]
                                .unsqueeze(2).broadcast_to((128, HEADS, HP)), ALU.mult)
                v.tensor_tensor(prev[:, :], tmpb[:, :],
                                states_sb[:, c * DIN:(c + 1) * DIN], ALU.add)
            svc.inc(b.vbar(v))
        chain_v = svc.n

        @blk.scalar
        def _(s, chain_v=chain_v):
            s.wait_ge(svc.h, chain_v)
            sc.inc(s.dma_start(out=b.cc2_in.ap(), in_=prev[:, :]), 16)
        cc2_wr = sc.n

        @blk.gpsimd
        def _(gp, cc2_wr=cc2_wr):
            gp.wait_ge(sc.h, cc2_wr)
            scl.inc(gp.collective_compute(
                "AllGather", ALU.bypass, replica_groups=GROUPS,
                ins=[b.cc2_in.ap()], outs=[b.cc2_out.ap()]))
            gp.wait_ge(scl.h, scl.n)
            sgq.inc(gp.dma_start(out=prev[:, :], in_=b.cc2_out.ap()[0:128, :]), 16)
        cc2_rd = sgq.n

        @blk.vector
        def _(v, cc2_rd=cc2_rd):
            v.wait_ge(sgq.h, cc2_rd)
            svc.inc(v.tensor_scalar(prev[:, :], prev[:, :], b.bnd[:, 0:1], None, ALU.mult))
    b.prev_init_v = svc.n

    if b.debug:
        with nc.Block() as blk:
            @blk.scalar
            def _(s):
                s.wait_ge(svc.h, b.prev_init_v)
                sc.inc(s.dma_start(out=b.dbg["states"][:, :], in_=states_sb[:, :]), 16)
                sc.inc(s.dma_start(out=b.dbg["prev4"][:, :], in_=prev[:, :]), 16)

    b.phase_gate()
    bcE = None
    Etile = sb("Etiles", [128, 2048], stk=ph_b)
    Lexp = sb("Lexps", [128, 2048], BF16, stk=ph_b)
    Mht = sb("Mhts", [128, 2 * 2048], BF16, stk=ph_b)
    hgt = sb("hgts", [128, 2 * DIN], BF16, stk=ph_b)
    sgrd = sb("sgrds", [128, 2 * 512], BF16, stk=ph_b)
    ynt = sb("ynts", [128, DIN], BF16, stk=ph_b)
    w1t = sb("w1ts", [128, 512], stk=ph_b)
    w3t = sb("w3ts", [128, 512], BF16, stk=ph_b)
    ph_b2 = ExitStack()
    ps_bc = ps("ps_bcs", [128, 2048], stk=ph_b2)
    ps_yd = ps("ps_yds", [128, 1024], stk=ph_b2)
    ps_yo = ps("ps_yos", [128, 1024], stk=ph_b2)

    b.hg_rows = {}
    evr = []
    sgrd_used = []
    for c in range(NCH):
        with nc.Block() as blk:
            @blk.sync
            def _(sync, c=c):
                sync.wait_ge(sc.h, b.actfix_done)
                if c >= 1:
                    sync.wait_ge(svc.h, evr[-1])
                for sh in range(2):
                    sy.inc(sync.dma_start_transpose(
                        xact_tm[:, sh * DIN:(sh + 1) * DIN],
                        b.act_dram[0:DIN, c * CH + sh * 128:c * CH + (sh + 1) * 128]), 16)
            xr = sy.n

            @blk.vector
            def _(v, c=c, xr=xr):
                v.wait_ge(sy.h, xr)
                for sh in range(2):
                    tof = (2 * c + sh) * HEADS
                    svc.inc(v.tensor_tensor(
                        xd_tm[:, sh * DIN:(sh + 1) * DIN].rearrange("p (h q) -> p h q", q=HP),
                        xact_tm[:, sh * DIN:(sh + 1) * DIN].rearrange("p (h q) -> p h q", q=HP),
                        b.dt_tm[:, tof:tof + HEADS].unsqueeze(2).broadcast_to((128, HEADS, HP)),
                        ALU.mult))
        xd_rdy = svc.n

        for g in range(8):
            pg = g % 2
            with nc.Block() as blk:
                @blk.scalar
                def _(s, c=c, g=g):
                    if (c, g) != (0, 0):
                        s.wait_ge(stn.h, stn.n)
                    sc.inc(s.dma_start(
                        out=b.flatbuf[0:1, (g % 2) * 2048:(g % 2) * 2048 + 2048]
                        .rearrange("o (h t) -> o h t", t=CH),
                        in_=b.dacum_dram[g * 8:(g + 1) * 8, c * CH:(c + 1) * CH]
                        .unsqueeze(0)), 16)
                fl_rd = sc.n

                @blk.tensor
                def _(t, c=c, g=g, fl_rd=fl_rd):
                    t.wait_ge(sc.h, fl_rd)
                    if (c, g) != (0, 0):
                        t.wait_ge(svc.h, svc.n)
                    fof = (g % 2) * 2048
                    for q in range(4):
                        last = t.matmul(ps_bc[:, q * 512:(q + 1) * 512], b.ones2[:, :],
                                        b.flatbuf[:, fof + q * 512:fof + (q + 1) * 512],
                                        start=True, stop=True)
                    stn.inc(last)
                bc_t = stn.n

                for sh in range(2):
                    @blk.vector
                    def _(v, c=c, g=g, sh=sh, bc_t=bc_t):
                        v.wait_ge(stn.h, bc_t)
                        if sh == 0 and (c, g) != (0, 0):
                            v.wait_ge(ssc.h, ssc.n)
                        tof = (2 * c + sh) * HEADS
                        svc.inc(v.tensor_tensor(
                            Etile[:, :].rearrange("p (h t) -> p h t", t=CH),
                            ps_bc[:, :].rearrange("p (h t) -> p h t", t=CH),
                            b.dacum_tm[:, tof + g * 8:tof + (g + 1) * 8]
                            .unsqueeze(2).broadcast_to((128, 8, CH)), ALU.subtract))
                    eg = svc.n

                    @blk.scalar
                    def _(s, eg=eg):
                        s.wait_ge(svc.h, eg)
                        s.activation(Lexp[:, :], Etile[:, :], AF.Exp)
                        ssc.inc(b.sbar(s))
                    ex = ssc.n

                    @blk.vector
                    def _(v, c=c, g=g, sh=sh, ex=ex):
                        v.wait_ge(ssc.h, ex)
                        svc.inc(v.scalar_tensor_tensor(
                            Mht[:, sh * 2048:(sh + 1) * 2048].rearrange("p (h t) -> p h t", t=CH),
                            Lexp[:, :].rearrange("p (h t) -> p h t", t=CH), 1.0,
                            b.gm_all[:, c * 512 + sh * 256:c * 512 + (sh + 1) * 256]
                            .unsqueeze(1).broadcast_to((128, 8, CH)),
                            ALU.min, ALU.mult))
                mh_rdy = svc.n

                for th in range(2):
                    pr = (2 * g + th) % 2
                    @blk.sync
                    def _(sync, c=c, g=g, th=th):
                        sync.wait_ge(sc.h, b.sg_written)
                        nread = len(sgrd_used)
                        if nread >= 2:
                            sync.wait_ge(svc.h, sgrd_used[-2])
                        r0 = c * CH + th * 128
                        sy.inc(sync.dma_start(
                            out=sgrd[:, (nread % 2) * 512:(nread % 2) * 512 + 512],
                            in_=b.sg_dram[r0:r0 + 128, g * 512:(g + 1) * 512]), 16)
                    sg_r = sy.n

                    @blk.tensor
                    def _(t, c=c, g=g, th=th, mh_rdy=mh_rdy, xd_rdy=xd_rdy, pr=pr):
                        t.wait_ge(svc.h, mh_rdy)
                        t.wait_ge(svc.h, xd_rdy)
                        t.wait_ge(svc.h, b.prev_init_v)
                        if len(evr) >= 2:
                            t.wait_ge(svc.h, evr[-2])
                        shs = [0] if th == 0 else [0, 1]
                        for hh in range(8):
                            for sh in shs:
                                last = t.matmul(
                                    ps_yd[:, pr * 512 + hh * HP:pr * 512 + (hh + 1) * HP],
                                    Mht[:, sh * 2048 + hh * 256 + th * 128:
                                        sh * 2048 + hh * 256 + th * 128 + 128],
                                    xd_tm[:, sh * DIN + (g * 8 + hh) * HP:
                                          sh * DIN + (g * 8 + hh + 1) * HP],
                                    start=(sh == shs[0]), stop=(sh == shs[-1]))
                        stn.inc(last)
                        stn.inc(t.matmul(
                            ps_yo[:, pr * 512:(pr + 1) * 512],
                            b.bmcm[:, T + c * CH + th * 128:T + c * CH + (th + 1) * 128],
                            prev[:, g * 512:(g + 1) * 512], start=True, stop=True))
                    ymm = stn.n

                    @blk.vector
                    def _(v, c=c, g=g, th=th, ymm=ymm, sg_r=sg_r, pr=pr):
                        v.wait_ge(stn.h, ymm)
                        v.wait_ge(sy.h, sg_r)
                        v.wait_ge(sgq.h, b.bweights_ld)
                        if g == 0 and c >= 1:
                            v.wait_ge(ssc.h, b.epi_ssc)
                        tof = (2 * c + th) * HEADS
                        v.tensor_tensor(
                            w1t[:, :].rearrange("p (h q) -> p h q", q=HP),
                            ps_yo[:, pr * 512:(pr + 1) * 512].rearrange("p (h q) -> p h q", q=HP),
                            b.eda_tm[:, tof + g * 8:tof + (g + 1) * 8]
                            .unsqueeze(2).broadcast_to((128, 8, HP)), ALU.mult)
                        v.tensor_tensor(w1t[:, :], w1t[:, :],
                                        ps_yd[:, pr * 512:(pr + 1) * 512], ALU.add)
                        v.tensor_tensor(w3t[:, :],
                                        xact_tm[:, th * DIN + g * 512:th * DIN + (g + 1) * 512],
                                        b.drep[:, g * 512:(g + 1) * 512], ALU.mult)
                        v.tensor_tensor(w1t[:, :], w1t[:, :], w3t[:, :], ALU.add)
                        nread = len(sgrd_used)
                        svc.inc(v.tensor_tensor(
                            hgt[:, th * DIN + g * 512:th * DIN + (g + 1) * 512], w1t[:, :],
                            sgrd[:, (nread % 2) * 512:(nread % 2) * 512 + 512], ALU.mult))
                        sgrd_used.append(svc.n)
                    evr.append(svc.n)

        with nc.Block() as blk:
            @blk.vector
            def _(v, c=c):
                v.wait_ge(stn.h, stn.n)
                v.tensor_tensor(tmpb[:, :].rearrange("p (h q) -> p h q", q=HP),
                                prev[:, :].rearrange("p (h q) -> p h q", q=HP),
                                b.etcall[:, c * HEADS:(c + 1) * HEADS]
                                .unsqueeze(2).broadcast_to((128, HEADS, HP)), ALU.mult)
                svc.inc(v.tensor_tensor(prev[:, :], tmpb[:, :],
                                        states_sb[:, c * DIN:(c + 1) * DIN], ALU.add))
        prev_upd = svc.n

        for th in range(2):
            with nc.Block() as blk:
                @blk.scalar
                def _(s, c=c, th=th):
                    s.wait_ge(svc.h, evr[-1] if th == 1 else evr[-2])
                    s.wait_ge(svc.h, svc.n)
                    s.wait_ge(sc.h, sc.n)
                    s.activation(ynt[:, :], hgt[:, th * DIN:(th + 1) * DIN],
                                 AF.Square, bias=0.0, scale=1.0,
                                 accum_out=b.rmsc[:, 0:1])
                    ssc.inc(b.sbar(s))
                sq_s = ssc.n

                @blk.vector
                def _(v, sq_s=sq_s):
                    v.wait_ge(ssc.h, sq_s)
                    v.tensor_scalar(b.rmsc[:, 2:3], b.rmsc[:, 0:1], 1.0 / DIN, EPS,
                                    ALU.mult, ALU.add)
                    v.drain()
                    v.reciprocal(b.rmsc[:, 3:4], b.rmsc[:, 2:3])
                    v.drain()
                    svc.inc(v.tensor_scalar(b.rmsc[:, 2:3], b.rmsc[:, 2:3], 1.0, None, ALU.mult))
                rc_v = svc.n

                @blk.scalar
                def _(s, rc_v=rc_v):
                    s.wait_ge(svc.h, rc_v)
                    ssc.inc(s.activation(b.rmsc[:, 1:2], b.rmsc[:, 3:4], AF.Sqrt))
                rst_s = ssc.n

                @blk.vector
                def _(v, c=c, th=th, rst_s=rst_s):
                    v.wait_ge(ssc.h, rst_s)
                    v.tensor_scalar(ynt[:, :], hgt[:, th * DIN:(th + 1) * DIN],
                                    b.rmsc[:, 1:2], None, ALU.mult)
                    svc.inc(v.tensor_tensor(ynt[:, :], ynt[:, :], b.normwrep[:, :], ALU.mult))
                yn_v = svc.n

                @blk.scalar
                def _(s, c=c, th=th, yn_v=yn_v):
                    s.wait_ge(svc.h, yn_v)
                    r0 = c * CH + th * 128
                    sc.inc(s.dma_start(out=b.yn_dram[r0:r0 + 128, :], in_=ynt[:, :]), 16)
                b.hg_rows[(c, th)] = sc.n
            b.epi_ssc = ssc.n

            if b.debug:
                with nc.Block() as blk:
                    @blk.scalar
                    def _(s, c=c, th=th):
                        s.wait_ge(svc.h, svc.n)
                        r0 = c * CH + th * 128
                        sc.inc(s.dma_start(out=b.dbg["hg"][r0:r0 + 128, :],
                                           in_=hgt[:, th * DIN:(th + 1) * DIN]), 16)
                        sc.inc(s.dma_start(out=b.dbg["yn"][r0:r0 + 128, :], in_=ynt[:, :]), 16)

    ph_b2.close()
    b.yn_all = sc.n
    ph_b.close()


def _phase_c(b):
    nc, sb, ps = b.nc, b.sb, b.ps
    sy, sc, sgq, stn, svc, ssc = b.sy, b.sc, b.sgq, b.stn, b.svc, b.ssc
    b.phase_gate()
    ph_c = ExitStack()
    ynfm = sb("ynfms", [128, 32 * 512], BF16, stk=ph_c)
    wop_sb = sb("wop_sbs", [128, 2 * 32 * 512], BF16, stk=ph_c)
    x_t = sb("x_ts", [128, 4 * D], stk=ph_c)
    x2t = sb("x2ts", [128, 4 * D], stk=ph_c)
    lnsc = sb("lnscs", [128, D], stk=ph_c)
    xq2st = sb("xq2sts", [128, D], BF16, stk=ph_c)
    ccol = sb("ccols", [128, 16], stk=ph_c)
    ph_pc = ExitStack()
    ps_c = ps("ps_cs", [128, 4096], stk=ph_pc)

    wop_ld = []
    opmm = []
    opev = []
    with nc.Block() as blk:
        @blk.gpsimd
        def _(gp):
            sgq.inc(gp.dma_start(out=wop_sb[:, 0:16384], in_=b.wop_d[0]), 16)
            wop_ld.append(sgq.n)

    b.ln2st = []
    for half in range(2):
        with nc.Block() as blk:
            @blk.sync
            def _(sync, half=half):
                sync.wait_ge(sc.h, b.yn_all)
                if half == 1:
                    sync.wait_ge(svc.h, svc.n)
                for k in range(32):
                    sy.inc(sync.dma_start_transpose(
                        ynfm[:, k * 512:(k + 1) * 512],
                        b.yn_dram[half * 512:(half + 1) * 512, k * 128:(k + 1) * 128]), 16)
                for tq in range(4):
                    sy.inc(sync.dma_start(out=x_t[:, tq * D:(tq + 1) * D],
                                          in_=b.x_d[half * 512 + tq * 128:half * 512 + (tq + 1) * 128, :]), 16)
            yn_rd = sy.n

            for dblk in range(4):
                pd = dblk % 2

                @blk.gpsimd
                def _(gp, half=half, dblk=dblk):
                    nxt = half * 4 + dblk + 1
                    if nxt < 8:
                        d2 = nxt % 4
                        if len(opmm) >= 1:
                            gp.wait_ge(stn.h, opmm[-1])
                        sgq.inc(gp.dma_start(
                            out=wop_sb[:, (nxt % 2) * 16384:(nxt % 2 + 1) * 16384],
                            in_=b.wop_d[d2]), 16)
                        wop_ld.append(sgq.n)

                @blk.tensor
                def _(t, half=half, dblk=dblk, pd=pd, yn_rd=yn_rd):
                    t.wait_ge(sy.h, yn_rd)
                    t.wait_ge(sgq.h, wop_ld[half * 4 + dblk])
                    li = len(opev)
                    if li >= 2:
                        t.wait_ge(svc.h, opev[li - 2])
                    wof = ((half * 4 + dblk) % 2) * 16384
                    pof = pd * 2048
                    for k in range(32):
                        for tq in range(4):
                            last = t.matmul(
                                ps_c[:, pof + tq * 512:pof + (tq + 1) * 512],
                                ynfm[:, k * 512 + tq * 128:k * 512 + (tq + 1) * 128],
                                wop_sb[:, wof + k * 512:wof + (k + 1) * 512],
                                start=(k == 0), stop=(k == 31))
                    stn.inc(last)
                opmm.append(stn.n)

                @blk.vector
                def _(v, half=half, dblk=dblk, pd=pd):
                    v.wait_ge(stn.h, opmm[-1])
                    if half == 1 and dblk == 0 and b.ln2st:
                        v.wait_ge(sc.h, b.ln2st[-1])
                    pof = pd * 2048
                    for tq in range(4):
                        v.tensor_tensor(x2t[:, tq * D + dblk * 512:tq * D + (dblk + 1) * 512],
                                        x_t[:, tq * D + dblk * 512:tq * D + (dblk + 1) * 512],
                                        ps_c[:, pof + tq * 512:pof + (tq + 1) * 512], ALU.add)
                    svc.inc(b.vbar(v))
                opev.append(svc.n)

            for tq in range(4):
                tcg = half * 4 + tq
                o8 = tq * 4

                @blk.scalar
                def _(s, half=half, tq=tq, tcg=tcg):
                    s.wait_ge(svc.h, opev[-1])
                    if b.ln2st:
                        s.wait_ge(sc.h, b.ln2st[-1])
                    sc.inc(s.dma_start(out=b.x2_dram[tcg * 128:(tcg + 1) * 128, :],
                                       in_=x2t[:, tq * D:(tq + 1) * D]), 16)
                    s.activation(lnsc[:, :], x2t[:, tq * D:(tq + 1) * D], AF.Square,
                                 bias=0.0, scale=1.0, accum_out=ccol[:, o8 + 2:o8 + 3])
                    ssc.inc(b.sbar(s))
                sqd = ssc.n

                @blk.vector
                def _(v, half=half, tq=tq, tcg=tcg, sqd=sqd):
                    xs = x2t[:, tq * D:(tq + 1) * D]
                    v.wait_ge(ssc.h, sqd)
                    v.reduce_sum(ccol[:, o8:o8 + 1], xs, AX.X)
                    v.drain()
                    v.tensor_scalar(ccol[:, o8 + 1:o8 + 2], ccol[:, o8:o8 + 1], 1.0 / D, None, ALU.mult)
                    v.tensor_scalar(ccol[:, o8 + 2:o8 + 3], ccol[:, o8 + 2:o8 + 3], 1.0 / D, None, ALU.mult)
                    v.drain()
                    v.tensor_tensor(ccol[:, o8 + 3:o8 + 4], ccol[:, o8 + 1:o8 + 2],
                                    ccol[:, o8 + 1:o8 + 2], ALU.mult)
                    v.drain()
                    v.tensor_tensor(ccol[:, o8 + 2:o8 + 3], ccol[:, o8 + 2:o8 + 3],
                                    ccol[:, o8 + 3:o8 + 4], ALU.subtract)
                    v.drain()
                    v.tensor_scalar(ccol[:, o8 + 3:o8 + 4], ccol[:, o8 + 2:o8 + 3], EPS, None, ALU.add)
                    v.drain()
                    v.reciprocal(ccol[:, o8 + 2:o8 + 3], ccol[:, o8 + 3:o8 + 4])
                    svc.inc(b.vbar(v))
                vd = svc.n

                @blk.scalar
                def _(s, vd=vd):
                    s.wait_ge(svc.h, vd)
                    s.activation(ccol[:, o8 + 3:o8 + 4], ccol[:, o8 + 2:o8 + 3], AF.Sqrt)
                    ssc.inc(b.sbar(s))
                rsd = ssc.n

                @blk.vector
                def _(v, half=half, tq=tq, tcg=tcg, rsd=rsd):
                    xs = x2t[:, tq * D:(tq + 1) * D]
                    v.wait_ge(ssc.h, rsd)
                    if b.ln2st:
                        v.wait_ge(sc.h, b.ln2st[-1])
                    v.scalar_tensor_tensor(lnsc[:, :], xs, ccol[:, o8 + 1:o8 + 2],
                                           ccol[:, o8 + 3:o8 + 4].broadcast_to((128, D)),
                                           ALU.subtract, ALU.mult)
                    v.drain()
                    v.reduce_max(ccol[:, o8:o8 + 1], lnsc[:, :], AX.X, apply_absolute_value=True)
                    v.drain()
                    v.tensor_scalar(ccol[:, o8 + 1:o8 + 2], ccol[:, o8:o8 + 1], EPS, None, ALU.max)
                    v.drain()
                    v.tensor_scalar(b.sx2c[:, tcg:tcg + 1], ccol[:, o8 + 1:o8 + 2],
                                    1.0 / 127.0, None, ALU.mult)
                    v.reciprocal(ccol[:, o8 + 2:o8 + 3], ccol[:, o8 + 1:o8 + 2])
                    v.drain()
                    v.tensor_scalar(ccol[:, o8 + 3:o8 + 4], ccol[:, o8 + 2:o8 + 3], 127.0, None, ALU.mult)
                    v.drain()
                    v.tensor_scalar(lnsc[:, :], lnsc[:, :], ccol[:, o8 + 3:o8 + 4], MAGIC,
                                    ALU.mult, ALU.add)
                    v.drain()
                    svc.inc(v.tensor_scalar(xq2st[:, :], lnsc[:, :], MAGIC, None, ALU.subtract))
                qd = svc.n

                @blk.scalar
                def _(s, tcg=tcg, qd=qd):
                    s.wait_ge(svc.h, qd)
                    sc.inc(s.dma_start(out=b.xq2_dram[tcg * 128:(tcg + 1) * 128, :],
                                       in_=xq2st[:, :]), 16)
                b.ln2st.append(sc.n)

        if b.debug:
            with nc.Block() as blk:
                @blk.scalar
                def _(s, half=half):
                    s.wait_ge(sc.h, b.ln2st[-1])
                    for tq in range(4):
                        tcg = half * 4 + tq
                        sc.inc(s.dma_start(out=b.dbg["x2"][tcg * 128:(tcg + 1) * 128, :],
                                           in_=b.x2_dram[tcg * 128:(tcg + 1) * 128, :]), 16)
                        sc.inc(s.dma_start(out=b.dbg["xq2"][tcg * 128:(tcg + 1) * 128, :],
                                           in_=b.xq2_dram[tcg * 128:(tcg + 1) * 128, :]), 16)
    ph_pc.close()
    ph_c.close()
    b.xq2_written = sc.n


def _phase_d(b):
    nc, sb, ps = b.nc, b.sb, b.ps
    sy, sc, sgq, stn, svc, ssc = b.sy, b.sc, b.sgq, b.stn, b.svc, b.ssc
    b.phase_gate()
    ph_d = ExitStack()
    xq2_fm = sb("xq2_fms", [128, NKD * T], BF16, stk=ph_d)
    wfs = sb("wfss", [128, 2 * 8192], stk=ph_d)
    wq = sb("wqs", [128, 2 * 8192], BF16, stk=ph_d)
    sigt = sb("sigts", [128, 2 * 2048], stk=ph_d)
    ht = sb("hts", [128, 2 * 2048], stk=ph_d)
    rmaxc = sb("rmaxcs", [128, 16], stk=ph_d)
    sxgc = sb("sxgcs", [128, NTC], stk=ph_d)
    sxuc = sb("sxucs", [128, NTC], stk=ph_d)
    ph_pd = ExitStack()
    ps_f = ps("ps_fs", [128, 4096], stk=ph_pd)

    with nc.Block() as blk:
        @blk.sync
        def _(sync):
            sync.wait_ge(sc.h, b.xq2_written)
            for k in range(NKD):
                sy.inc(sync.dma_start_transpose(
                    xq2_fm[:, k * T:(k + 1) * T],
                    b.xq2_dram[:, k * 128:(k + 1) * 128]), 16)
        xq2f_rd = sy.n

        @blk.vector
        def _(v):
            v.wait_ge(svc.h, b.wq_ready)
            v.memset(rmaxc[:, :], 0.0)
            v.tensor_scalar(sxgc[:, :], b.sx2c[:, :], b.wscol[:, 0:1], None, ALU.mult)
            v.tensor_scalar(sxuc[:, :], b.sx2c[:, :], b.wscol[:, 1:2], None, ALU.mult)
            svc.inc(b.vbar(v))
        b.sx_prep = svc.n

    wload = []
    wqd = []
    fmm = []
    sevac = []
    hevac = []
    hwr = []

    def wslab_load(sync, ob, w):
        j = 2 * ob + w
        if j - 2 in wqd_d:
            sync.wait_ge(svc.h, wqd_d[j - 2])
        src = b.wg_d if w == 0 else b.wu_d
        sy.inc(sync.dma_start(out=wfs[:, (j % 2) * 8192:(j % 2 + 1) * 8192], in_=src[ob]), 16)
        wld_d[j] = sy.n

    def wslab_quant(v, ob, w):
        j = 2 * ob + w
        v.wait_ge(sy.h, wld_d[j])
        v.wait_ge(svc.h, b.sx_prep)
        if j >= 2 and (j - 2) in slab_mm:
            v.wait_ge(stn.h, slab_mm[j - 2])
        wof = (j % 2) * 8192
        wcol = b.wqcol[:, w:w + 1]
        v.tensor_scalar(wfs[:, wof:wof + 8192], wfs[:, wof:wof + 8192], wcol, 1.0,
                        ALU.mult, ALU.min)
        v.tensor_scalar(wfs[:, wof:wof + 8192], wfs[:, wof:wof + 8192], -1.0, MAGIC,
                        ALU.max, ALU.add)
        v.drain()
        v.tensor_scalar(wq[:, wof:wof + 8192], wfs[:, wof:wof + 8192], MAGIC, None,
                        ALU.subtract)
        svc.inc(b.vbar(v))
        wqd_d[j] = svc.n

    wld_d = {}
    wqd_d = {}
    slab_mm = {}
    sevac = {}
    hevac = {}
    hwr = []
    with nc.Block() as blk:
        @blk.sync
        def _(sync):
            wslab_load(sync, 0, 0)
            wslab_load(sync, 0, 1)

        @blk.vector
        def _(v):
            wslab_quant(v, 0, 0)
            wslab_quant(v, 0, 1)

    for ob in range(16):
        with nc.Block() as blk:
            @blk.sync
            def _(sync, ob=ob):
                if ob + 1 < 16:
                    wslab_load(sync, ob + 1, 0)
                    wslab_load(sync, ob + 1, 1)

            for w in range(2):
                for tch in range(2):
                    @blk.tensor
                    def _(t, ob=ob, w=w, tch=tch):
                        j = 2 * ob + w
                        t.wait_ge(svc.h, wqd_d[j])
                        t.wait_ge(sy.h, xq2f_rd)
                        if w == 0:
                            if (ob - 1, tch) in hevac:
                                t.wait_ge(svc.h, hevac[(ob - 1, tch)])
                        else:
                            t.wait_ge(ssc.h, sevac[(ob, tch)])
                        wof = (j % 2) * 8192
                        for k in range(NKD):
                            for tq in range(4):
                                tcc = tch * 4 + tq
                                last = t.matmul(
                                    ps_f[:, tcc * 512:(tcc + 1) * 512],
                                    xq2_fm[:, k * T + tcc * 128:k * T + (tcc + 1) * 128],
                                    wq[:, wof + k * 512:wof + (k + 1) * 512],
                                    start=(k == 0), stop=(k == NKD - 1))
                        stn.inc(last)
                    if tch == 1:
                        slab_mm[2 * ob + w] = stn.n
                    fm_cur = stn.n

                    if w == 0:
                        @blk.scalar
                        def _(s, ob=ob, tch=tch, fm_cur=fm_cur):
                            s.wait_ge(stn.h, fm_cur)
                            if (ob - 1, tch) in hevac:
                                s.wait_ge(svc.h, hevac[(ob - 1, tch)])
                            sof = tch * 2048
                            for tq in range(4):
                                tcc = tch * 4 + tq
                                s.activation(sigt[:, sof + tq * 512:sof + (tq + 1) * 512],
                                             ps_f[:, tcc * 512:(tcc + 1) * 512],
                                             AF.Sigmoid, bias=0.0,
                                             scale=sxgc[:, tcc:tcc + 1])
                            ssc.inc(b.sbar(s))
                        sevac[(ob, tch)] = ssc.n
                    else:
                        @blk.vector
                        def _(v, ob=ob, tch=tch, fm_cur=fm_cur):
                            v.wait_ge(stn.h, fm_cur)
                            v.wait_ge(ssc.h, sevac[(ob, tch)])
                            if len(hwr) >= 2:
                                v.wait_ge(sc.h, hwr[-2])
                            hof = (tch % 2) * 2048
                            sof = tch * 2048
                            for tq in range(4):
                                tcc = tch * 4 + tq
                                v.scalar_tensor_tensor(
                                    ht[:, hof + tq * 512:hof + (tq + 1) * 512],
                                    ps_f[:, tcc * 512:(tcc + 1) * 512],
                                    sxuc[:, tcc:tcc + 1],
                                    sigt[:, sof + tq * 512:sof + (tq + 1) * 512], ALU.mult, ALU.mult)
                                v.drain()
                                v.reduce_max(rmaxc[:, 8 + tcc:9 + tcc],
                                             ht[:, hof + tq * 512:hof + (tq + 1) * 512],
                                             AX.X, apply_absolute_value=True)
                                v.drain()
                                v.tensor_tensor(rmaxc[:, tcc:tcc + 1], rmaxc[:, tcc:tcc + 1],
                                                rmaxc[:, 8 + tcc:9 + tcc], ALU.max)
                            svc.inc(b.vbar(v))
                        hevac[(ob, tch)] = svc.n

                        @blk.scalar
                        def _(s, ob=ob, tch=tch):
                            s.wait_ge(svc.h, hevac[(ob, tch)])
                            hof = (tch % 2) * 2048
                            for tq in range(4):
                                tcc = tch * 4 + tq
                                sc.inc(s.dma_start(
                                    out=b.h_dram[tcc * 128:(tcc + 1) * 128, ob * 512:(ob + 1) * 512],
                                    in_=ht[:, hof + tq * 512:hof + (tq + 1) * 512]), 16)
                        hwr.append(sc.n)

            @blk.vector
            def _(v, ob=ob):
                if ob + 1 < 16:
                    wslab_quant(v, ob + 1, 0)
                    wslab_quant(v, ob + 1, 1)

    with nc.Block() as blk:
        @blk.vector
        def _(v):
            v.wait_ge(svc.h, svc.n)
            v.tensor_scalar(rmaxc[:, 0:8], rmaxc[:, 0:8], EPS, None, ALU.max)
            v.drain()
            v.tensor_scalar(b.sx2c[:, 0:8], rmaxc[:, 0:8], 1.0 / 127.0, None, ALU.mult)
            v.drain()
            v.reciprocal(rmaxc[:, 8:16], rmaxc[:, 0:8])
            v.drain()
            v.tensor_scalar(b.invshc[:, 0:8], rmaxc[:, 8:16], 127.0, None, ALU.mult)
            svc.inc(b.vbar(v))
        b.shf = svc.n

        @blk.scalar
        def _(s):
            s.wait_ge(svc.h, b.shf)
            for tc in range(NTC):
                sc.inc(s.dma_start(out=b.sh_dram[tc:tc + 1, :].rearrange("o f -> f o"),
                                   in_=b.sx2c[:, tc:tc + 1]), 16)
        b.sh_wr = sc.n
    if b.debug:
        with nc.Block() as blk:
            @blk.scalar
            def _(s):
                s.wait_ge(sc.h, sc.n)
                for tc in range(NTC):
                    sc.inc(s.dma_start(out=b.dbg["h"][tc * 128:(tc + 1) * 128, :],
                                       in_=b.h_dram[tc * 128:(tc + 1) * 128, :]), 16)
    ph_pd.close()
    ph_d.close()
    b.h_all = sc.n


def _phase_e(b):
    nc, sb, ps = b.nc, b.sb, b.ps
    sy, sc, sgq, stn, svc, ssc = b.sy, b.sc, b.sgq, b.stn, b.svc, b.ssc
    b.phase_gate()
    ph_e = ExitStack()
    hq_fm = sb("hq_fms", [128, 64 * T], BF16, stk=ph_e)

    ph_q = ExitStack()
    hrt = sb("hrts", [128, 2 * D], stk=ph_q)
    hqt = sb("hqts", [128, 2 * D], BF16, stk=ph_q)
    hrd = []
    hqv = []
    hqw = []
    for tc in range(NTC):
        for oq in range(4):
            j = tc * 4 + oq
            pj = j % 2
            with nc.Block() as blk:
                @blk.sync
                def _(sync, tc=tc, oq=oq, j=j, pj=pj):
                    sync.wait_ge(sc.h, b.h_all)
                    if j >= 2:
                        sync.wait_ge(svc.h, hqv[j - 2])
                    sy.inc(sync.dma_start(out=hrt[:, pj * D:(pj + 1) * D],
                                          in_=b.h_dram[tc * 128:(tc + 1) * 128,
                                                       oq * D:(oq + 1) * D]), 16)
                hrd.append(sy.n)

                @blk.vector
                def _(v, tc=tc, oq=oq, j=j, pj=pj):
                    v.wait_ge(sy.h, hrd[j])
                    v.wait_ge(svc.h, b.shf)
                    if j >= 2:
                        v.wait_ge(sc.h, hqw[j - 2])
                    v.tensor_scalar(hrt[:, pj * D:(pj + 1) * D], hrt[:, pj * D:(pj + 1) * D],
                                    b.invshc[:, tc:tc + 1], MAGIC, ALU.mult, ALU.add)
                    v.drain()
                    v.tensor_scalar(hqt[:, pj * D:(pj + 1) * D], hrt[:, pj * D:(pj + 1) * D],
                                    MAGIC, None, ALU.subtract)
                    svc.inc(b.vbar(v))
                hqv.append(svc.n)

                @blk.scalar
                def _(s, tc=tc, oq=oq, j=j, pj=pj):
                    s.wait_ge(svc.h, hqv[j])
                    sc.inc(s.dma_start(out=b.hq_dram[tc * 128:(tc + 1) * 128,
                                                     oq * D:(oq + 1) * D],
                                       in_=hqt[:, pj * D:(pj + 1) * D]), 16)
                hqw.append(sc.n)
    ph_q.close()
    hq_all = sc.n
    if b.debug:
        with nc.Block() as blk:
            @blk.scalar
            def _(s):
                s.wait_ge(sc.h, hq_all)
                for tc in range(NTC):
                    sc.inc(s.dma_start(out=b.dbg["hq"][tc * 128:(tc + 1) * 128, :],
                                       in_=b.hq_dram[tc * 128:(tc + 1) * 128, :]), 16)

    with nc.Block() as blk:
        @blk.sync
        def _(sync):
            sync.wait_ge(sc.h, hq_all)
            for k in range(64):
                sy.inc(sync.dma_start_transpose(
                    hq_fm[:, k * T:(k + 1) * T],
                    b.hq_dram[:, k * 128:(k + 1) * 128]), 16)
    hqf_rd = sy.n

    shrow = sb("shrows", [2, 1024], stk=ph_e)
    shw_bc = sb("shw_bcs", [128, 1024], stk=ph_e)
    ph_bb = ExitStack()
    ps_b2 = ps("ps_b2s", [128, 1024], stk=ph_bb)
    with nc.Block() as blk:
        @blk.vector
        def _(v):
            svc.inc(v.memset(shrow[:, :], 0.0))
        z1 = svc.n

        @blk.scalar
        def _(s, z1=z1):
            s.wait_ge(svc.h, z1)
            s.wait_ge(sc.h, b.sh_wr)
            sc.inc(s.dma_start(out=shrow[0:1, :],
                               in_=b.sh_dram.ap().rearrange("a b -> () (a b)")), 16)
        shr = sc.n

        @blk.vector
        def _(v, shr=shr):
            v.wait_ge(sc.h, shr)
            v.tensor_scalar(shrow[0:1, :], shrow[0:1, :], b.wsc1[0:1, 2:3], None, ALU.mult)
            svc.inc(b.vbar(v))
        shs = svc.n

        @blk.tensor
        def _(t, shs=shs):
            t.wait_ge(svc.h, shs)
            for q in range(2):
                last = t.matmul(ps_b2[:, q * 512:(q + 1) * 512], b.ones2[:, :],
                                shrow[:, q * 512:(q + 1) * 512], start=True, stop=True)
            stn.inc(last)
        sht = stn.n

        @blk.vector
        def _(v, sht=sht):
            v.wait_ge(stn.h, sht)
            svc.inc(v.tensor_copy(shw_bc[:, :], ps_b2[:, :]))
    ph_bb.close()
    shbc_v = svc.n

    wdf = sb("wdfs", [128, 2 * 1024], stk=ph_e)
    wdq = sb("wdqs", [128, 2 * 1024], BF16, stk=ph_e)
    dnt = sb("dnts", [128, 2 * 1024], BF16, stk=ph_e)
    ph_pe = ExitStack()
    ps_dn = ps("ps_dns", [128, 2 * 1024], stk=ph_pe)
    dld = []
    dqd = []
    dmm = []
    dnv = []
    dnw = []
    NPC = 8
    for d in range(16):
        for kk in range(64 // NPC):
            j = d * (64 // NPC) + kk
            pj = j % 2
            with nc.Block() as blk:
                @blk.sync
                def _(sync, d=d, kk=kk, j=j, pj=pj):
                    if j >= 2:
                        sync.wait_ge(stn.h, dmm[j - 2])
                    sy.inc(sync.dma_start(
                        out=wdf[:, pj * 1024:(pj + 1) * 1024],
                        in_=b.wd_d[d][:, kk * NPC * 128:(kk + 1) * NPC * 128]), 16)
                dld.append(sy.n)

                @blk.vector
                def _(v, d=d, kk=kk, j=j, pj=pj):
                    v.wait_ge(sy.h, dld[j])
                    v.wait_ge(svc.h, b.wq_ready)
                    if j >= 2:
                        v.wait_ge(stn.h, dmm[j - 2])
                    wof = pj * 1024
                    v.tensor_scalar(wdf[:, wof:wof + 1024], wdf[:, wof:wof + 1024],
                                    b.wqcol[:, 2:3], 1.0, ALU.mult, ALU.min)
                    v.tensor_scalar(wdf[:, wof:wof + 1024], wdf[:, wof:wof + 1024],
                                    -1.0, MAGIC, ALU.max, ALU.add)
                    v.drain()
                    v.tensor_scalar(wdq[:, wof:wof + 1024], wdf[:, wof:wof + 1024],
                                    MAGIC, None, ALU.subtract)
                    svc.inc(b.vbar(v))
                dqd.append(svc.n)

                @blk.tensor
                def _(t, d=d, kk=kk, j=j, pj=pj):
                    t.wait_ge(svc.h, dqd[j])
                    t.wait_ge(sy.h, hqf_rd)
                    if kk == 0 and d >= 2:
                        t.wait_ge(svc.h, dnv[d - 2])
                    wof = pj * 1024
                    pof = (d % 2) * 1024
                    for ki in range(NPC):
                        kg = kk * NPC + ki
                        t.matmul(ps_dn[:, pof:pof + 512],
                                 wdq[:, wof + ki * 128:wof + (ki + 1) * 128],
                                 hq_fm[:, kg * T:kg * T + 512],
                                 start=(kg == 0), stop=(kg == 63))
                        last = t.matmul(ps_dn[:, pof + 512:pof + 1024],
                                        wdq[:, wof + ki * 128:wof + (ki + 1) * 128],
                                        hq_fm[:, kg * T + 512:(kg + 1) * T],
                                        start=(kg == 0), stop=(kg == 63))
                    stn.inc(last)
                dmm.append(stn.n)

        with nc.Block() as blk:
            @blk.vector
            def _(v, d=d):
                v.wait_ge(stn.h, dmm[-1])
                v.wait_ge(svc.h, shbc_v)
                if len(dnw) >= 2:
                    v.wait_ge(sc.h, dnw[-2])
                pof = (d % 2) * 1024
                v.tensor_tensor(dnt[:, pof:pof + 1024], ps_dn[:, pof:pof + 1024],
                                shw_bc[:, :], ALU.mult)
                svc.inc(b.vbar(v))
            dnv.append(svc.n)

            @blk.scalar
            def _(s, d=d):
                s.wait_ge(svc.h, dnv[-1])
                pof = (d % 2) * 1024
                sc.inc(s.dma_start(out=b.dn_dram[d * 128:(d + 1) * 128, :],
                                   in_=dnt[:, pof:pof + 1024]), 16)
            dnw.append(sc.n)
    ph_pe.close()
    dn_all = sc.n

    x2r = sb("x2rs", [128, 2 * D], stk=ph_e)
    ffr = sb("ffrs", [128, 2 * D], BF16, stk=ph_e)
    ord_ = []
    ov = []
    ow = []
    for tc in range(NTC):
        pj = tc % 2
        with nc.Block() as blk:
            @blk.sync
            def _(sync, tc=tc, pj=pj):
                sync.wait_ge(sc.h, dn_all)
                if tc >= 2:
                    sync.wait_ge(svc.h, ov[tc - 2])
                sy.inc(sync.dma_start(out=x2r[:, pj * D:(pj + 1) * D],
                                      in_=b.x2_dram[tc * 128:(tc + 1) * 128, :]), 16)
                sy.inc(sync.dma_start_transpose(
                    ffr[:, pj * D:(pj + 1) * D],
                    b.dn_dram[:, tc * 128:(tc + 1) * 128]), 16)
            ord_.append(sy.n)

            @blk.vector
            def _(v, tc=tc, pj=pj):
                v.wait_ge(sy.h, ord_[tc])
                if tc >= 2:
                    v.wait_ge(sc.h, ow[tc - 2])
                v.tensor_tensor(x2r[:, pj * D:(pj + 1) * D], x2r[:, pj * D:(pj + 1) * D],
                                ffr[:, pj * D:(pj + 1) * D], ALU.add)
                svc.inc(b.vbar(v))
            ov.append(svc.n)

            @blk.scalar
            def _(s, tc=tc, pj=pj):
                s.wait_ge(svc.h, ov[tc])
                sc.inc(s.dma_start(out=b.out_d[tc * 128:(tc + 1) * 128, :],
                                   in_=x2r[:, pj * D:(pj + 1) * D]), 16)
            ow.append(sc.n)
    ph_e.close()
    b.out_done = True
    b.out_final = sc.n


def _finish(b, phases):
    nc = b.nc
    sy, sc, sgq, stn, svc, ssc = b.sy, b.sc, b.sgq, b.stn, b.svc, b.ssc
    if getattr(b, "out_done", False):
        with nc.Block() as blk:
            @blk.sync
            def _(sync):
                sync.wait_ge(sc.h, b.out_final)
                sync.wait_ge(sy.h, sy.n)
                sync.wait_ge(sgq.h, sgq.n)
        return
    zt = b.st.enter_context(nc.sbuf_tensor("zouts", [128, D], F32))
    consume = [b.ones2, b.identf, b.gmask, b.convw, b.convb, b.dtbias, b.alog, b.drep,
               b.normw, b.bnd, b.acol, b.flatbuf, b.etcrow, b.dacum_tm, b.dt_tm,
               b.eda_tm, b.ddc_tm, b.etcall, b.bmcm, b.bm_tm, b.gm_all, b.sx2c,
               b.wqcol, b.wsc1, b.abscol, b.onescol, b.rmsc]
    with nc.Block() as blk:
        @blk.vector
        def _(v):
            v.wait_ge(svc.h, svc.n)
            v.wait_ge(ssc.h, ssc.n)
            v.wait_ge(stn.h, stn.n)
            v.memset(zt[:, :], 0.0)
            for tns in consume:
                ap = tns.ap()
                p, f = ap.shape
                f = min(f, 16)
                try:
                    v.tensor_tensor(zt[0:p, 0:f], zt[0:p, 0:f], tns[0:p, 0:f], ALU.add)
                except Exception:
                    pass
            b.svc.inc(v.tensor_scalar(zt[:, :], zt[:, :], 0.0, None, ALU.mult))

        @blk.sync
        def _(sync):
            sync.wait_ge(b.svc.h, b.svc.n)
            sync.wait_ge(b.sc.h, b.sc.n)
            sync.wait_ge(b.sgq.h, b.sgq.n)
            for tc in range(NTC):
                b.sy.inc(sync.dma_start(out=b.out_d[tc * 128:(tc + 1) * 128, :], in_=zt[:, :]), 16)
            sync.wait_ge(b.sy.h, b.sy.n)


def _stage_tiles(wt, nk, no, osz):
    K, O = wt.shape
    a = wt.reshape(nk, 128, no, osz).transpose(2, 1, 0, 3).reshape(no, 128, nk * osz)
    return np.ascontiguousarray(a)


def _stage(inputs):
    ip_w = np.asarray(inputs["L0_attn_in_proj_w"])
    conv_w = np.asarray(inputs["L0_attn_conv_w"])
    conv_b = np.asarray(inputs["L0_attn_conv_b"])
    dt_bias = np.asarray(inputs["L0_attn_dt_bias"])
    A_log = np.asarray(inputs["L0_attn_A_log"])
    Dv = np.asarray(inputs["L0_attn_D"])
    norm_w = np.asarray(inputs["L0_attn_norm_w"])
    op_w = np.asarray(inputs["L0_attn_out_proj_w"])
    w_gate = np.asarray(inputs["L0_ffn_w_gate"])
    w_up = np.asarray(inputs["L0_ffn_w_up"])
    w_down = np.asarray(inputs["L0_ffn_w_down"])

    WT = np.ascontiguousarray(ip_w.T)
    com = {}
    com["wgate"] = _stage_tiles(WT[:, 0:DIN], NKD, 8, 512)
    com["wx"] = _stage_tiles(WT[:, DIN:DIN + CONVD], NKD, NOC, 128)
    wdt = WT[:, DIN + CONVD:]
    com["wdt"] = np.ascontiguousarray(
        wdt.reshape(NKD, 128, HP).transpose(1, 0, 2).reshape(128, NKD * HP))
    com["wop"] = _stage_tiles(np.ascontiguousarray(op_w.T), 32, 4, 512)
    com["wg"] = _stage_tiles(np.ascontiguousarray(w_gate.T), NKD, 16, 512)
    com["wu"] = _stage_tiles(np.ascontiguousarray(w_up.T), NKD, 16, 512)
    com["wd"] = _stage_tiles(np.ascontiguousarray(w_down.T), 64, 16, 128)
    com["convw"] = np.ascontiguousarray(
        conv_w.reshape(NOC, 128, 4).transpose(1, 0, 2).reshape(128, NOC * 4))
    com["convb"] = np.ascontiguousarray(conv_b.reshape(NOC, 128).T)
    com["dtbias"] = np.ascontiguousarray(dt_bias.reshape(HEADS, 1))
    com["alog"] = np.ascontiguousarray(A_log.reshape(HEADS, 1))
    com["drep"] = np.ascontiguousarray(
        np.broadcast_to(np.repeat(Dv, HP)[None, :], (128, DIN)))
    com["normw"] = np.ascontiguousarray(norm_w.reshape(32, 128).T)
    com["normwrep"] = np.ascontiguousarray(np.broadcast_to(norm_w[None, :], (128, DIN)))
    o2 = np.zeros((2, 128), np.float32)
    o2[0] = 1.0
    com["ones2"] = o2
    com["identf"] = np.eye(128, dtype=np.float32)
    gm = np.zeros((128, 512), bf16)
    for sh in range(2):
        s_idx = np.arange(128)[:, None] + sh * 128
        t_idx = np.arange(256)[None, :]
        gm[:, sh * 256:(sh + 1) * 256] = (t_idx >= s_idx).astype(bf16)
    com["gmask"] = gm
    return com


def kernel(**inputs):
    x = np.asarray(inputs["L0_attn_hidden_states"])
    com = _stage(inputs)
    nc, _ = _build(debug=False)
    in_maps = []
    for core in range(8):
        bb, half = core // 2, core % 2
        m = dict(com)
        m["x"] = np.ascontiguousarray(x[bb, half * T:(half + 1) * T, :])
        bn = np.full((128, 1), 1.0 if half == 1 else 0.0, np.float32)
        m["bnd"] = bn
        in_maps.append(m)
    res = run_bass_kernel_spmd(nc, in_maps, core_ids=list(range(8)))
    out = np.empty((4, 2048, 2048), np.float32)
    for core in range(8):
        bb, half = core // 2, core % 2
        out[bb, half * T:(half + 1) * T, :] = np.asarray(res.results[core]["out"])
    return out


if __name__ == "__main__":
    nc, _ = _build(debug=True, phases=("a",))
    print("built ok")
